# version 05 (final)
# baseline (speedup 1.0000x reference)
import os
import numpy as np
import ml_dtypes
from contextlib import ExitStack

import concourse.bass as bass
import concourse.mybir as mybir
from concourse.bass_core_rust import IndirectOffsetOnAxis
from concourse.bass_utils import run_bass_kernel_spmd

F32 = mybir.dt.float32
BF16 = mybir.dt.bfloat16
I32 = mybir.dt.int32
U8 = mybir.dt.uint8
AF = mybir.ActivationFunctionType
ALU = mybir.AluOpType
AX = mybir.AxisListType.X

H = 2048
NMLP = 16384
TC = 1024
NTT = 8
HCN = 16
DKV = 512
HD = 64
NQH = 32
NKVH = 8
SLICE = 2048
NCH = 32

ENGS = ("sync", "tensor", "vector", "scalar", "gpsimd")


def _dt_size(dt):
    return {F32: 4, BF16: 2, I32: 4}[dt]


def build(phases=("ffn1", "attn", "ffn2"), dbg=False):
    nc = bass.Bass("TRN2", target_bir_lowering=False)

    x_in = nc.dram_tensor("x", [TC, H], F32, kind="ExternalInput").ap()
    wq_in = nc.dram_tensor("wq", [H, H], F32, kind="ExternalInput").ap()
    wk_in = nc.dram_tensor("wk", [H, DKV], F32, kind="ExternalInput").ap()
    wv_in = nc.dram_tensor("wv", [H, DKV], F32, kind="ExternalInput").ap()
    wo_in = nc.dram_tensor("wo", [H, H], F32, kind="ExternalInput").ap()
    gpk_in = nc.dram_tensor("gpk", [SLICE, 1024], I32, kind="ExternalInput").ap()
    gsc_in = nc.dram_tensor("gsc", [SLICE, 64], F32, kind="ExternalInput").ap()
    upk_in = nc.dram_tensor("upk", [SLICE, 1024], I32, kind="ExternalInput").ap()
    usc_in = nc.dram_tensor("usc", [SLICE, 64], F32, kind="ExternalInput").ap()
    dpk_in = nc.dram_tensor("dpk", [H, 1024], I32, kind="ExternalInput").ap()
    dsc_in = nc.dram_tensor("dsc", [H, 64], F32, kind="ExternalInput").ap()
    mkt_in = nc.dram_tensor("mkt", [8, 8, 128, 4, 128], BF16, kind="ExternalInput").ap()
    id_in = nc.dram_tensor("ident", [128, 128], BF16, kind="ExternalInput").ap()
    cst_in = nc.dram_tensor("cst", [128, 16], F32, kind="ExternalInput").ap()
    kidx_in = nc.dram_tensor("kidx", [128, 32], I32, kind="ExternalInput").ap()
    vidx_in = nc.dram_tensor("vidx", [128, 32], I32, kind="ExternalInput").ap()
    out_t = nc.dram_tensor("out", [TC, H], F32, kind="ExternalOutput").ap()
    if dbg == "attn":
        dat_oT = nc.dram_tensor("dat_oT", [128, HCN * TC], BF16, kind="ExternalOutput").ap()
        dat_vx = nc.dram_tensor("dat_vx", [128, 32 * 8 * 65], BF16, kind="ExternalOutput").ap()
        dat_kg = nc.dram_tensor("dat_kg", [8 * DKV, TC], BF16, kind="ExternalOutput").ap()
        dat_vg = nc.dram_tensor("dat_vg", [8 * TC, DKV], BF16, kind="ExternalOutput").ap()
    if dbg:
        dx2 = nc.dram_tensor("dx2", [TC, H], F32, kind="ExternalOutput").ap()
        dcg = nc.dram_tensor("dcg", [SLICE, 2048], BF16, kind="ExternalOutput").ap()
        dcd = nc.dram_tensor("dcd", [SLICE, 2048], BF16, kind="ExternalOutput").ap()
        dgg = nc.dram_tensor("dgg", [SLICE, 2048], BF16, kind="ExternalOutput").ap()

    cg = nc.dram_tensor("cg", [SLICE, 2048], BF16, kind="Internal").ap()
    cu = nc.dram_tensor("cu", [SLICE, 2048], BF16, kind="Internal").ap()
    cd = nc.dram_tensor("cd", [SLICE, 2048], BF16, kind="Internal").ap()
    Gg = nc.dram_tensor("Gg", [NMLP, 2048], BF16, kind="Internal", addr_space="Shared").ap()
    Gu = nc.dram_tensor("Gu", [NMLP, 2048], BF16, kind="Internal", addr_space="Shared").ap()
    Gd = nc.dram_tensor("Gd", [NMLP, 2048], BF16, kind="Internal", addr_space="Shared").ap()
    kc_d = nc.dram_tensor("kc_d", [DKV, TC], BF16, kind="Internal").ap()
    vc_d = nc.dram_tensor("vc_d", [TC, DKV], BF16, kind="Internal").ap()
    KG = nc.dram_tensor("KG", [8 * DKV, TC], BF16, kind="Internal").ap()
    VG = nc.dram_tensor("VG", [8 * TC, DKV], BF16, kind="Internal").ap()

    ctx = ExitStack()
    sbt = lambda name, shape, dt: ctx.enter_context(nc.sbuf_tensor(name, shape, dt))

    x_sb = sbt("x_sb", [128, NTT, H], F32)
    A1 = sbt("A1", [128, 32768], U8)
    A2 = sbt("A2", [128, 49152], U8)
    A3 = sbt("A3", [128, 49152], U8)
    A4 = sbt("A4", [128, 15616], U8)
    id_sb = sbt("id_sb", [128, 128], BF16)
    cst_sb = sbt("cst_sb", [128, 16], F32)
    kidx_sb = sbt("kidx_sb", [128, 32], I32)
    vidx_sb = sbt("vidx_sb", [128, 32], I32)
    CONSTS = [-4.0, -12.0, 1e-5, 2.0, 0.125, 0.5, -0.125, -8.0, 7.5]
    for _ci, _cv in enumerate(CONSTS):
        nc.const_aps.register(F32, _cv, cst_sb[:, _ci:_ci + 1])

    E = ctx.enter_context(nc.psum_tensor("E", [128, 1024], F32))
    C = ctx.enter_context(nc.psum_tensor("C", [128, 2048], F32))
    D0 = ctx.enter_context(nc.psum_tensor("D0", [128, 1024], BF16))
    D1 = ctx.enter_context(nc.psum_tensor("D1", [128, 1024], BF16))
    D = [D0, D1]

    SEM_NAMES = [
        "dx", "dpk0", "dpk1", "dq1", "pq1", "aq", "dq3", "pq2", "pdq", "dqe", "dstr",
        "cc", "lnv", "lna", "xnt", "wdma", "peg", "peu", "ped", "pet", "gel",
        "itr", "itev", "xadd", "wpd", "wcast", "pep", "qev", "kvout", "kd0", "kd1",
        "vkd0", "vkd1", "wdma0", "wdma1",
        "mkd", "pesc", "mh", "exp", "pept", "ptm", "pepv", "oev", "peot",
        "ooev", "dout",
    ]
    sems = {n: ctx.enter_context(nc.semaphore(n)) for n in SEM_NAMES}

    def carve(arena, off, dt, shape):
        n = 1
        for s in shape[1:]:
            n *= s
        nb = n * _dt_size(dt)
        ap = arena[:, off:off + nb].bitcast(dt)
        if len(shape) == 2:
            return ap
        dims = " ".join(f"d{i}" for i in range(1, len(shape)))
        kw = {f"d{i}": shape[i] for i in range(1, len(shape))}
        return ap.rearrange(f"p ({dims}) -> p {dims}", **kw)

    xnT = carve(A1, 0, BF16, [128, HCN, TC])

    dq_pk = [carve(A2, 4096 * i, I32, [128, 1024]) for i in range(2)]
    dq_sc = [carve(A2, 8192 + 256 * i, F32, [128, 64]) for i in range(2)]
    dq_scp = [carve(A2, 8704 + 256 * i, F32, [128, 64]) for i in range(2)]
    dq_u0 = carve(A2, 12288, F32, [128, 1024])
    dq_s0 = carve(A2, 16384, F32, [128, 1024])
    dq_m0 = carve(A2, 20480, F32, [128, 1024])
    dq_r40 = carve(A2, 24576, F32, [128, 1024])
    dq_r60 = carve(A2, 28672, F32, [128, 1024])
    dq_b0 = carve(A2, 32768, F32, [128, 1024])
    dq_il = carve(A2, 36864, I32, [128, 1024])
    dq_ih = carve(A2, 40960, I32, [128, 1024])
    dq_vals = [carve(A2, 45056 + 2048 * i, BF16, [128, 512, 2]) for i in range(2)]
    dq_u1 = carve(A3, 0, F32, [128, 1024])
    dq_s1 = carve(A3, 4096, F32, [128, 1024])
    dq_m1 = carve(A3, 8192, F32, [128, 1024])
    dq_r41 = carve(A3, 12288, F32, [128, 1024])
    dq_r61 = carve(A3, 16384, F32, [128, 1024])
    dq_b1 = carve(A3, 20480, F32, [128, 1024])
    dq_vals2 = [carve(A3, 24576 + 2048 * i, BF16, [128, 512, 2]) for i in range(2)]
    dq_strT = carve(A3, 28672, BF16, [128, 16, 512])

    ffn_wg = [carve(A2, 0, BF16, [128, 16, 512]), carve(A3, 0, BF16, [128, 16, 512])]
    ffn_wu = [carve(A2, 16384, BF16, [128, 16, 512]), carve(A3, 16384, BF16, [128, 16, 512])]
    ffn_wd = [carve(A2, 32768, BF16, [128, 4, 2048]), carve(A3, 32768, BF16, [128, 4, 2048])]

    qT = carve(A2, 0, BF16, [128, HCN, TC])
    k_sb = [carve(A2, 32768 + 8192 * i, BF16, [128, 4, 1024]) for i in range(2)]
    wst_f = carve(A3, 0, F32, [128, 16, 512])
    wst_b = carve(A3, 32768, BF16, [128, 16, 512])
    v_ext = carve(A3, 0, BF16, [128, 32, 8, 65])

    sqbuf = carve(A4, 0, F32, [128, 2048])
    mk_sb = carve(A4, 0, BF16, [128, 8, 4, 128])
    xn_tt = carve(A4, 8192, BF16, [128, 2048])
    p_sb = [carve(A4, 8192 + 1024 * i, BF16, [128, 512]) for i in range(2)]
    pt_sb = [carve(A4, 10240 + 1024 * i, BF16, [128, 4, 128]) for i in range(2)]
    stats = carve(A4, 12288, F32, [128, 16])
    s_ssum = stats[:, 0:1]
    s_ssq = stats[:, 1:2]
    s_mean = stats[:, 2:3]
    s_var = stats[:, 3:4]
    s_std = stats[:, 4:5]
    s_rstd = stats[:, 5:6]
    s_nmr = stats[:, 6:7]
    s_mstat = stats[:, 7:8]
    s_mbias = stats[:, 8:9]
    s_rec = stats[:, 9:10]
    qstage = carve(A4, 12352, BF16, [128, 512])
    ffn_g = carve(A4, 0, F32, [128, 512])
    ffn_i = carve(A4, 2048, BF16, [128, 512])
    ffn_iT = [carve(A4, 3072 + 1024 * i, BF16, [128, 4, 128]) for i in range(2)]
    vst = [carve(A4, 13440 + 1024 * i, BF16, [128, 512]) for i in range(2)]
    ob_sb = carve(A4, 15488, BF16, [128, 64])

    q = {e: [] for e in ENGS}
    cnt = {n: 0 for n in SEM_NAMES}

    def em(e, fn):
        q[e].append(fn)

    def wt(e, sem, val):
        if val > 0:
            em(e, lambda eng, s=sems[sem], v=val: eng.wait_ge(s, v))

    def dma(e, out_ap, in_ap, sem, **kw):
        em(e, lambda eng, o=out_ap, i=in_ap, s=sems[sem], k=dict(kw):
            eng.dma_start(out=o, in_=i, **k).then_inc(s, 16))
        cnt[sem] += 16
        return cnt[sem]

    def pub(e, sem):
        em(e, lambda eng, s=sems[sem]: eng.drain().then_inc(s, 1))
        cnt[sem] += 1
        return cnt[sem]

    def op(e, meth, *args, **kw):
        em(e, lambda eng, m=meth, a=args, k=kw: getattr(eng, m)(*a, **k))
        if e in ("vector", "scalar", "gpsimd"):
            em(e, lambda eng: eng.drain())

    def mm(out_ap, lhsT, rhs, start, stop, inc=None, sgc=False):
        if inc is None:
            em("tensor", lambda eng, o=out_ap, l=lhsT, r=rhs, st=start, sp=stop,
                g=sgc: eng.matmul(o, l, r, start=st, stop=sp, skip_group_check=g))
            return None
        em("tensor", lambda eng, o=out_ap, l=lhsT, r=rhs, st=start, sp=stop,
            g=sgc, s=sems[inc]:
            eng.matmul(o, l, r, start=st, stop=sp, skip_group_check=g).then_inc(s, 1))
        cnt[inc] += 1
        return cnt[inc]

    def dguard():
        for s in ("xnt", "itev", "dqe", "ptm", "ooev"):
            wt("tensor", s, cnt[s])

    def tr(out_ap, in_ap, inc=None):
        if inc is None:
            em("tensor", lambda eng, o=out_ap, i=in_ap, d=id_sb[:]:
                eng.transpose(o, i, d))
            return None
        em("tensor", lambda eng, o=out_ap, i=in_ap, d=id_sb[:], s=sems[inc]:
            eng.transpose(o, i, d).then_inc(s, 1))
        cnt[inc] += 1
        return cnt[inc]

    dma("sync", x_sb[:], x_in.rearrange("(t p) h -> p t h", p=128), "dx")
    dma("sync", id_sb[:], id_in, "dx")
    dma("sync", cst_sb[:], cst_in, "dx")
    dma("sync", kidx_sb[:], kidx_in, "dx")
    dma("sync", vidx_sb[:], vidx_in, "dx")
    wt("scalar", "dx", 80)

    weights_dq = [
        (gpk_in, gsc_in, cg, Gg, False),
        (upk_in, usc_in, cu, Gu, False),
        (dpk_in, dsc_in, cd, Gd, True),
    ]
    wt("tensor", "dx", 80)
    ped_hist_dq = []
    dq1_hist = []
    dqe_hist = []
    dstr_hist = []
    for wi, (pkA, scA, contrib, _, fold) in enumerate(weights_dq):
        for t in range(16):
            it = wi * 16 + t
            slot = it % 2
            if it >= 2:
                wt("sync", "dq1", dq1_hist[it - 2])
            dma("sync", dq_pk[slot], pkA[128 * t:128 * (t + 1), :], f"dpk{slot}")
            dma("sync", dq_sc[slot], scA[128 * t:128 * (t + 1), :], f"dpk{slot}")
            wt("vector", f"dpk{slot}", cnt[f"dpk{slot}"])
            op("vector", "tensor_scalar", dq_scp[slot], dq_sc[slot],
               0.5 if fold else 1.0, None, ALU.mult)
            op("vector", "tensor_scalar", dq_il, dq_pk[slot], 15, None, ALU.bitwise_and)
            op("vector", "tensor_scalar", dq_ih, dq_pk[slot], 4, 15,
               ALU.logical_shift_right, ALU.bitwise_and)
            dq1_hist.append(pub("vector", "dq1"))
            if it >= 2:
                wt("vector", "pdq", ped_hist_dq[it - 2])
                wt("gpsimd", "pdq", ped_hist_dq[it - 2])
            op("vector", "tensor_copy", dq_u0, dq_il)
            op("vector", "tensor_scalar", dq_s0, dq_u0, 7.5, None, ALU.is_gt)
            op("vector", "scalar_tensor_tensor", dq_m0, dq_s0, -8.0, dq_u0,
               ALU.mult, ALU.add)
            m0pub = pub("vector", "dq3")
            op("vector", "tensor_copy", dq_u1, dq_ih)
            op("vector", "tensor_scalar", dq_s1, dq_u1, 7.5, None, ALU.is_gt)
            op("vector", "scalar_tensor_tensor", dq_m1, dq_s1, -8.0, dq_u1,
               ALU.mult, ALU.add)
            m1pub = pub("vector", "pq1")
            wt("scalar", "dq3", m0pub)
            op("scalar", "activation", dq_r40, dq_m0, AF.Relu, -4.0, 1.0)
            op("scalar", "activation", dq_r60, dq_m0, AF.Relu, -12.0, 2.0)
            wt("scalar", "pq1", m1pub)
            op("scalar", "activation", dq_r41, dq_m1, AF.Relu, -4.0, 1.0)
            op("scalar", "activation", dq_r61, dq_m1, AF.Relu, -12.0, 2.0)
            aqn = pub("scalar", "aq")
            wt("vector", "aq", aqn)
            op("vector", "scalar_tensor_tensor", dq_b0, dq_r60, 1.0, dq_r40,
               ALU.mult, ALU.add)
            op("vector", "tensor_tensor", dq_b0, dq_b0, dq_m0, ALU.add)
            op("vector", "tensor_scalar", dq_s0, dq_s0, -1.0, 0.5, ALU.mult, ALU.add)
            op("vector", "tensor_tensor", dq_b0, dq_b0, dq_s0, ALU.mult)
            op("vector", "tensor_tensor",
               dq_vals[slot][:, :, 0].rearrange("p (b i) -> p b i", i=16),
               dq_b0[:, 0:512].rearrange("p (b i) -> p b i", i=16),
               dq_scp[slot][:, 0:32].rearrange("p (b o) -> p b o", o=1).broadcast_to([128, 32, 16]),
               ALU.mult)
            op("vector", "tensor_tensor",
               dq_vals2[slot][:, :, 0].rearrange("p (b i) -> p b i", i=16),
               dq_b0[:, 512:1024].rearrange("p (b i) -> p b i", i=16),
               dq_scp[slot][:, 32:64].rearrange("p (b o) -> p b o", o=1).broadcast_to([128, 32, 16]),
               ALU.mult)
            v0pub = pub("vector", "dq3")
            wt("vector", "aq", aqn)
            op("vector", "scalar_tensor_tensor", dq_b1, dq_r61, 1.0, dq_r41,
               ALU.mult, ALU.add)
            op("vector", "tensor_tensor", dq_b1, dq_b1, dq_m1, ALU.add)
            op("vector", "tensor_scalar", dq_s1, dq_s1, -1.0, 0.5, ALU.mult, ALU.add)
            op("vector", "tensor_tensor", dq_b1, dq_b1, dq_s1, ALU.mult)
            op("vector", "tensor_tensor",
               dq_vals[slot][:, :, 1].rearrange("p (b i) -> p b i", i=16),
               dq_b1[:, 0:512].rearrange("p (b i) -> p b i", i=16),
               dq_scp[slot][:, 0:32].rearrange("p (b o) -> p b o", o=1).broadcast_to([128, 32, 16]),
               ALU.mult)
            op("vector", "tensor_tensor",
               dq_vals2[slot][:, :, 1].rearrange("p (b i) -> p b i", i=16),
               dq_b1[:, 512:1024].rearrange("p (b i) -> p b i", i=16),
               dq_scp[slot][:, 32:64].rearrange("p (b o) -> p b o", o=1).broadcast_to([128, 32, 16]),
               ALU.mult)
            v1pub = pub("vector", "pq2")
            wt("tensor", "dq3", v0pub)
            wt("tensor", "pq2", v1pub)
            dguard()
            vflat = [dq_vals[slot].rearrange("p b i -> p (b i)"),
                     dq_vals2[slot].rearrange("p b i -> p (b i)")]
            for g in range(2):
                for kc8 in range(8):
                    kc = g * 8 + kc8
                    src = vflat[kc // 8][:, (kc % 8) * 128:(kc % 8 + 1) * 128]
                    last = (kc8 == 7)
                    tr(D[g][:, kc8 * 128:(kc8 + 1) * 128], src,
                       inc="pdq" if last else None)
            ped_hist_dq.append(cnt["pdq"])
            wt("vector", "pdq", cnt["pdq"])
            if t % 4 == 0 and len(dstr_hist) > 0:
                wt("vector", "dstr", dstr_hist[-1])
            op("vector", "tensor_copy",
               dq_strT[:, 0:8, 128 * (t % 4):128 * (t % 4 + 1)],
               D[0][:].rearrange("p (a b) -> p a b", a=8))
            op("vector", "tensor_copy",
               dq_strT[:, 8:16, 128 * (t % 4):128 * (t % 4 + 1)],
               D[1][:].rearrange("p (a b) -> p a b", a=8))
            dqe_hist.append(pub("vector", "dqe"))
            if t % 4 == 3:
                st = t // 4
                wt("sync", "dqe", dqe_hist[-1])
                dstr_hist.append(dma(
                    "sync",
                    contrib[:, 512 * st:512 * (st + 1)].rearrange(
                        "(kc p) n -> p kc n", p=128),
                    dq_strT[:], "dstr"))
    wt("gpsimd", "dstr", cnt["dstr"])
    for (_, _, contrib, G, _) in weights_dq:
        em("gpsimd", lambda eng, i=contrib, o=G:
            eng.collective_compute(
                "AllGather", ALU.bypass,
                replica_groups=[[0, 1, 2, 3, 4, 5, 6, 7]],
                ins=[i.opt()], outs=[o.opt()],
            ).then_inc(sems["cc"], 1))
        cnt["cc"] += 1
    cc_w = cnt["cc"]

    if dbg == "full":
        wt("sync", "dstr", cnt["dstr"])
        dma("sync", dcg, cg, "dout")
        dma("sync", dcd, cd, "dout")
        wt("sync", "cc", cc_w)
        dma("sync", dgg, Gg[0:SLICE, :], "dout")

    def emit_ln(x_waits, tr_waits):
        for tt in range(NTT):
            xap = x_sb[:, tt, :]
            for (s, v) in x_waits:
                wt("vector", s, v)
                wt("scalar", s, v)
            if tt == 0:
                for (s, v) in tr_waits:
                    wt("scalar", s, v)
            op("vector", "reduce_sum", s_ssum, xap, axis=AX)
            lv = pub("vector", "lnv")
            op("scalar", "activation", sqbuf, xap, AF.Square, 0.0, 1.0, 0.0, s_ssq)
            la = pub("scalar", "lna")
            wt("vector", "lna", la)
            op("vector", "tensor_scalar", s_mean, s_ssum, 1.0 / H, None, ALU.mult)
            op("vector", "tensor_tensor", s_var, s_mean, s_mean, ALU.mult)
            op("vector", "scalar_tensor_tensor", s_var, s_ssq, 1.0 / H, s_var,
               ALU.mult, ALU.subtract)
            lv = pub("vector", "lnv")
            wt("scalar", "lnv", lv)
            op("scalar", "activation", s_std, s_var, AF.Sqrt, 1e-5, 1.0)
            la = pub("scalar", "lna")
            wt("vector", "lna", la)
            op("vector", "reciprocal", s_rstd, s_std)
            op("vector", "scalar_tensor_tensor", s_nmr, s_mean, -1.0, s_rstd,
               ALU.mult, ALU.mult)
            lv = pub("vector", "lnv")
            wt("vector", "lnv", lv)
            if tt > 0:
                wt("vector", "pet", cnt["pet"])
            op("vector", "tensor_scalar", xn_tt, xap, s_rstd, s_nmr,
               ALU.mult, ALU.add)
            la = pub("vector", "lnv")
            wt("tensor", "lnv", la)
            if tt == 0:
                for (s, v) in tr_waits:
                    wt("tensor", s, v)
            dguard()
            for g in range(2):
                for i8 in range(8):
                    hc = g * 8 + i8
                    tr(D[g][:, i8 * 128:(i8 + 1) * 128],
                       xn_tt[:, hc * 128:(hc + 1) * 128],
                       inc="pet" if i8 == 7 else None)
            wt("vector", "pet", cnt["pet"])
            op("vector", "tensor_copy", xnT[:, 0:8, 128 * tt:128 * (tt + 1)],
               D[0][:].rearrange("p (a b) -> p a b", a=8))
            op("vector", "tensor_copy", xnT[:, 8:16, 128 * tt:128 * (tt + 1)],
               D[1][:].rearrange("p (a b) -> p a b", a=8))
            pub("vector", "xnt")
        return cnt["xnt"]

    def emit_ffn(Gg_, Gu_, Gd_, wdma_free_waits):
        ped_it = []
        xnt_req = cnt["xnt"]
        for it in range(NCH):
            s_, r_ = it // 8, it % 8
            nc_ = 4 * r_ + s_
            slot = it % 2
            if it < 2:
                for (s, v) in wdma_free_waits:
                    wt("sync", s, v)
            if it >= 2:
                wt("sync", "ped", ped_it[it - 2])
            dma("sync", ffn_wg[slot][:],
                Gg_[2048 * r_:2048 * (r_ + 1), 512 * s_:512 * (s_ + 1)]
                .rearrange("(hc p) n -> p hc n", p=128), f"wdma{slot}")
            dma("sync", ffn_wu[slot][:],
                Gu_[2048 * r_:2048 * (r_ + 1), 512 * s_:512 * (s_ + 1)]
                .rearrange("(hc p) n -> p hc n", p=128), f"wdma{slot}")
            wend = dma("sync", ffn_wd[slot][:],
                       Gd_[512 * nc_:512 * (nc_ + 1), :]
                       .rearrange("(sub p) h -> p sub h", p=128), f"wdma{slot}")
            for tt in range(NTT):
                j = it * NTT + tt
                wt("tensor", f"wdma{slot}", wend)
                if it == 0:
                    wt("tensor", "xnt", xnt_req)
                if os.environ.get("FFN_SERIAL"):
                    wt("tensor", "xadd", cnt["xadd"])
                wt("tensor", "gel", cnt["gel"])
                for hc in range(HCN):
                    mm(E[:, 0:512], xnT[:, hc, 128 * tt:128 * (tt + 1)],
                       ffn_wg[slot][:, hc, :], start=(hc == 0), stop=(hc == 15),
                       inc="peg" if hc == 15 else None)
                wt("tensor", "itr", cnt["itr"])
                for hc in range(HCN):
                    mm(E[:, 512:1024], xnT[:, hc, 128 * tt:128 * (tt + 1)],
                       ffn_wu[slot][:, hc, :], start=(hc == 0), stop=(hc == 15),
                       inc="peu" if hc == 15 else None)
                wt("scalar", "peg", cnt["peg"])
                wt("scalar", "itr", cnt["itr"])
                op("scalar", "activation", ffn_g, E[:, 0:512], AF.Gelu_apprx_tanh)
                gl = pub("scalar", "gel")
                wt("vector", "peu", cnt["peu"])
                wt("vector", "gel", gl)
                wt("vector", "pet", cnt["pet"])
                op("vector", "tensor_tensor", ffn_i, ffn_g, E[:, 512:1024], ALU.mult)
                ipub = pub("vector", "itr")
                wt("tensor", "itr", ipub)
                dguard()
                for sub in range(4):
                    tr(D[j % 2][:, sub * 128:(sub + 1) * 128],
                       ffn_i[:, sub * 128:(sub + 1) * 128],
                       inc="pet" if sub == 3 else None)
                wt("vector", "pet", cnt["pet"])
                if j >= 2:
                    wt("vector", "ped", cnt["ped"] - 1)
                op("vector", "tensor_copy", ffn_iT[j % 2][:],
                   D[j % 2][:, 0:512].rearrange("p (a b) -> p a b", a=4))
                ie = pub("vector", "itev")
                wt("tensor", "itev", ie)
                wt("tensor", "xadd", cnt["xadd"])
                for ho in range(4):
                    for sub in range(4):
                        mm(C[:, 512 * ho:512 * (ho + 1)],
                           ffn_iT[j % 2][:, sub, :],
                           ffn_wd[slot][:, sub, 512 * ho:512 * (ho + 1)],
                           start=(sub == 0), stop=(sub == 3),
                           inc="ped" if (ho == 3 and sub == 3) else None)
                wt("vector", "ped", cnt["ped"])
                op("vector", "tensor_tensor", x_sb[:, tt, :], x_sb[:, tt, :],
                   C[:], ALU.add)
                pub("vector", "xadd")
            ped_it.append(cnt["ped"])
        return cnt["ped"], cnt["xadd"]

    ffn1_ped, ffn1_xadd = 0, 0
    if "ffn1" in phases:
        emit_ln([("dx", 80)], [])
        ffn1_ped, ffn1_xadd = emit_ffn(Gg, Gu, Gd, [("cc", cc_w)])

    if dbg == "full" and "ffn1" in phases:
        wt("sync", "xadd", cnt["xadd"])
        dma("sync", dx2.rearrange("(t p) h -> p t h", p=128), x_sb[:], "dout")

    do_attn = ("attn" in phases) or ("proj" in phases)
    proj_only = ("proj" in phases) and ("attn" not in phases)
    ATTN_QT = int(os.environ.get("ATTN_QT", "8"))
    if do_attn:
        emit_ln([("xadd", ffn1_xadd), ("dx", 80)], [("ped", ffn1_ped)])

    def proj_weight_load(w_in_, col0, first_waits):
        wt("sync", "wcast", cnt["wcast"])
        for (s, v) in first_waits:
            wt("sync", s, v)
        dend = dma("sync", wst_f[:],
                   w_in_[:, col0:col0 + 512].rearrange("(hc p) n -> p hc n", p=128),
                   "wpd")
        wt("vector", "wpd", dend)
        wt("vector", "pep", cnt["pep"])
        op("vector", "tensor_copy", wst_b[:], wst_f[:])
        return pub("vector", "wcast")

    if do_attn:
        NOPROJ = bool(os.environ.get("PROJ_NONE"))
        first_proj = True
        for half in range(0 if NOPROJ else 2):
            tts = [4 * half + i for i in range(4)]
            for dq_ in range(4):
                wc = proj_weight_load(
                    wq_in, 512 * dq_, [("ped", ffn1_ped)] if first_proj else [])
                first_proj = False
                wt("tensor", "wcast", wc)
                wt("tensor", "qev", cnt["qev"])
                wt("tensor", "xadd", cnt["xadd"])
                for hc in range(HCN):
                    for ti, tt in enumerate(tts):
                        mm(C[:, 512 * ti:512 * (ti + 1)],
                           xnT[:, hc, 128 * tt:128 * (tt + 1)], wst_b[:, hc, :],
                           start=(hc == 0), stop=(hc == 15), sgc=True,
                           inc="pep" if (hc == 15 and ti == 3) else None)
                for ti, tt in enumerate(tts):
                    wt("vector", "pep", cnt["pep"])
                    wt("vector", "pept", cnt["pept"])
                    wt("vector", "kvout", cnt["kvout"])
                    op("vector", "tensor_copy", qstage, C[:, 512 * ti:512 * (ti + 1)])
                    qs = pub("vector", "qev")
                    wt("tensor", "qev", qs)
                    dguard()
                    for sub in range(4):
                        tr(D[ti % 2][:, sub * 128:(sub + 1) * 128],
                           qstage[:, sub * 128:(sub + 1) * 128],
                           inc="pept" if sub == 3 else None)
                    wt("vector", "pept", cnt["pept"])
                    op("vector", "tensor_copy",
                       qT[:, 4 * dq_:4 * dq_ + 4, 128 * tt:128 * (tt + 1)],
                       D[ti % 2][:, 0:512].rearrange("p (a b) -> p a b", a=4))
                    pub("vector", "ooev")
            for (w_in_, is_k) in ((wk_in, True), (wv_in, False)):
                wc = proj_weight_load(w_in_, 0, [])
                wt("tensor", "wcast", wc)
                wt("tensor", "qev", cnt["qev"])
                wt("tensor", "xadd", cnt["xadd"])
                for hc in range(HCN):
                    for ti, tt in enumerate(tts):
                        mm(C[:, 512 * ti:512 * (ti + 1)],
                           xnT[:, hc, 128 * tt:128 * (tt + 1)], wst_b[:, hc, :],
                           start=(hc == 0), stop=(hc == 15), sgc=True,
                           inc="pep" if (hc == 15 and ti == 3) else None)
                for ti, tt in enumerate(tts):
                    wt("vector", "pep", cnt["pep"])
                    wt("vector", "pept", cnt["pept"])
                    wt("vector", "kvout", cnt["kvout"])
                    op("vector", "tensor_copy", qstage, C[:, 512 * ti:512 * (ti + 1)])
                    qs = pub("vector", "qev")
                    if is_k:
                        wt("tensor", "qev", qs)
                        dguard()
                        for sub in range(4):
                            tr(D[ti % 2][:, sub * 128:(sub + 1) * 128],
                               qstage[:, sub * 128:(sub + 1) * 128],
                               inc="pept" if sub == 3 else None)
                        wt("vector", "pept", cnt["pept"])
                        wt("vector", "kvout", cnt["kvout"])
                        op("vector", "tensor_copy", pt_sb[0].rearrange("p a b -> p (a b)"),
                           D[ti % 2][:, 0:512])
                        oo = pub("vector", "ooev")
                        wt("sync", "ooev", oo)
                        dma("sync",
                            kc_d[:, 128 * tt:128 * (tt + 1)]
                            .rearrange("(ds p) t -> p ds t", p=128),
                            pt_sb[0][:], "kvout")
                    else:
                        wt("sync", "qev", qs)
                        dma("sync", vc_d[128 * tt:128 * (tt + 1), :], qstage, "kvout")

        NOGATHER = bool(os.environ.get("PROJ_NOGATHER"))
        wt("gpsimd", "kvout", cnt["kvout"])
        for (ci, co) in (() if NOGATHER else ((kc_d, KG), (vc_d, VG))):
            em("gpsimd", lambda eng, i=ci, o=co:
                eng.collective_compute(
                    "AllGather", ALU.bypass,
                    replica_groups=[[0, 1, 2, 3, 4, 5, 6, 7]],
                    ins=[i.opt()], outs=[o.opt()],
                ).then_inc(sems["cc"], 1))
            cnt["cc"] += 1
        cc_kv = cnt["cc"]
        wt("gpsimd", "dx", 80)

        NOVEXT = bool(os.environ.get("PROJ_NOVEXT"))
        NOMEMSET = bool(os.environ.get("PROJ_NOMEMSET"))
        wt("sync", "cc", cc_kv)
        wt("vector", "pep", cnt["pep"])
        wt("gpsimd", "cc", cc_kv)
        for ch in range(0 if (NOGATHER or NOVEXT) else 32):
            if ch >= 2:
                wt("gpsimd", "pq2", cnt["pq2"])
            em("gpsimd", lambda eng, s=sems[f"vkd{ch % 2}"], dst=vst[ch % 2][:],
                ixap=vidx_sb[:, ch:ch + 1]:
                eng.indirect_dma_start(
                    dst, None, VG, IndirectOffsetOnAxis(ixap, 0)
                ).then_inc(s, 16))
            cnt[f"vkd{ch % 2}"] += 16
            wt("vector", f"vkd{ch % 2}", cnt[f"vkd{ch % 2}"])
            op("vector", "tensor_copy", v_ext[:, ch, :, 0:64],
               vst[ch % 2][:].rearrange("p (h d) -> p h d", h=8))
            pub("vector", "pq2")
        wt("vector", "pep", cnt["pep"])
        if not (NOVEXT or NOMEMSET or NOGATHER):
            op("vector", "memset", v_ext[:, :, :, 64:65], 1.0)
        vx_done = pub("vector", "pq2")
        if dbg == "attn":
            wt("sync", "pq2", vx_done)
            dma("sync", dat_vx, v_ext[:].rearrange("p a b c -> p (a b c)"), "dout")

        wt("vector", "cc", cc_kv)
        sc_hist = {}
        ptm_hist = []
        qt_list = [] if proj_only else list(range(ATTN_QT))
        kh_list = [] if proj_only else list(range(8))
        for kh in kh_list:
            kk0 = kh * len(qt_list)
            if kh >= 2:
                wt("gpsimd", "pesc", sc_hist[(kh - 2) * len(qt_list) + len(qt_list) - 1])
            for kb in range(4):
                em("gpsimd", lambda eng, s=sems[f"kd{kh % 2}"],
                    dst=k_sb[kh % 2][:, kb, :],
                    ixap=kidx_sb[:, 4 * kh + kb:4 * kh + kb + 1]:
                    eng.indirect_dma_start(
                        dst, None, KG, IndirectOffsetOnAxis(ixap, 0)
                    ).then_inc(s, 16))
                cnt[f"kd{kh % 2}"] += 16
            kend = cnt[f"kd{kh % 2}"]
            for qt in qt_list:
                kk = kh * len(qt_list) + qt
                if ptm_hist:
                    wt("sync", "ptm", ptm_hist[-1])
                mend = dma("sync", mk_sb[:],
                           mkt_in[qt].rearrange("kb b sub a -> b kb sub a"), "mkd")
                for hq in range(4):
                    h = 4 * kh + hq
                    g = (qt * 8 + kh) * 4 + hq
                    lhsT = qT[64 * (h % 2):64 * (h % 2) + 64, h // 2,
                              128 * qt:128 * (qt + 1)]
                    for j in range(8):
                        kb, hf = j // 2, j % 2
                        wt("tensor", "exp", max(0, cnt["exp"] - 1))
                        if j == 0:
                            wt("tensor", f"kd{kh % 2}", kend)
                            wt("tensor", "mh", cnt["mh"])
                            wt("tensor", "gel", cnt["gel"])
                            wt("tensor", "itr", cnt["itr"])
                            wt("tensor", "ooev", cnt["ooev"])
                        mm(E[:, 512 * (j % 2):512 * (j % 2) + 512], lhsT,
                           k_sb[kh % 2][64 * (h % 2):64 * (h % 2) + 64, kb,
                                        512 * hf:512 * (hf + 1)],
                           start=True, stop=True, inc="pesc")
                        if j == 0:
                            wt("vector", "pesc", cnt["pesc"])
                            wt("vector", "exp", cnt["exp"])
                            op("vector", "reduce_max", s_mstat, E[:, 0:512], axis=AX)
                            op("vector", "tensor_scalar", s_mbias, s_mstat, -0.125,
                               None, ALU.mult)
                            mh = pub("vector", "mh")
                            wt("scalar", "mh", mh)
                        wt("scalar", "pesc", cnt["pesc"])
                        wt("scalar", "pept", max(0, cnt["pept"] - 1))
                        op("scalar", "activation", p_sb[j % 2],
                           E[:, 512 * (j % 2):512 * (j % 2) + 512], AF.Exp,
                           s_mbias, 0.125)
                        ex = pub("scalar", "exp")
                        wt("tensor", "exp", ex)
                        dguard()
                        if j == 0:
                            wt("tensor", "mkd", mend)
                        for sub in range(4):
                            tr(D[j % 2][:, sub * 128:(sub + 1) * 128],
                               p_sb[j % 2][:, sub * 128:(sub + 1) * 128],
                               inc="pept" if sub == 3 else None)
                        wt("vector", "pept", cnt["pept"])
                        wt("vector", "pepv", max(0, cnt["pepv"] - 1))
                        op("vector", "tensor_tensor", pt_sb[j % 2][:],
                           D[j % 2][:, 0:512].rearrange("p (a b) -> p a b", a=4),
                           mk_sb[:, j, :, :], ALU.mult)
                        pm = pub("vector", "ptm")
                        wt("tensor", "ptm", pm)
                        if j == 0:
                            wt("tensor", "oev", cnt["oev"])
                            wt("tensor", "qev", cnt["qev"])
                            wt("tensor", "pq2", vx_done)
                        for sub in range(4):
                            mm(C[:, 0:65], pt_sb[j % 2][:, sub, :],
                               v_ext[:, 4 * j + sub, kh, :],
                               start=(j == 0 and sub == 0), stop=(j == 7 and sub == 3),
                               sgc=True, inc="pepv" if sub == 3 else None)
                    wt("vector", "pepv", cnt["pepv"])
                    op("vector", "reciprocal", s_rec, C[:, 64:65])
                    wt("vector", "peot", cnt["peot"])
                    op("vector", "tensor_scalar", ob_sb, C[:, 0:64], s_rec, None,
                       ALU.mult)
                    ov = pub("vector", "oev")
                    wt("tensor", "oev", ov)
                    dguard()
                    tr(D0[0:64, 0:128], ob_sb[:], inc="peot")
                    wt("vector", "peot", cnt["peot"])
                    op("vector", "tensor_copy",
                       xnT[64 * (h % 2):64 * (h % 2) + 64, h // 2,
                           128 * qt:128 * (qt + 1)],
                       D0[0:64, 0:128])
                    pub("vector", "ooev")
                sc_hist[kk] = cnt["pesc"]
                ptm_hist.append(cnt["ptm"])
        attn_end_pepv = cnt["pepv"]
        attn_end_ptm = cnt["ptm"]
        oT_done = cnt["ooev"]

        skip_wo = proj_only or ATTN_QT < 8
        first = True
        for half in range(0 if skip_wo else 2):
            tts = [4 * half + i for i in range(4)]
            for ho in range(4):
                fw = [("pepv", attn_end_pepv)] if first else []
                wc = proj_weight_load(wo_in, 512 * ho, fw)
                first = False
                wt("tensor", "wcast", wc)
                wt("tensor", "xadd", cnt["xadd"])
                wt("tensor", "oev", cnt["oev"])
                if half == 0 and ho == 0:
                    wt("tensor", "ooev", oT_done)
                for dc in range(HCN):
                    for ti, tt in enumerate(tts):
                        mm(C[:, 512 * ti:512 * (ti + 1)],
                           xnT[:, dc, 128 * tt:128 * (tt + 1)], wst_b[:, dc, :],
                           start=(dc == 0), stop=(dc == 15), sgc=True,
                           inc="pep" if (dc == 15 and ti == 3) else None)
                for ti, tt in enumerate(tts):
                    wt("vector", "pep", cnt["pep"])
                    op("vector", "tensor_tensor", x_sb[:, tt, 512 * ho:512 * (ho + 1)],
                       x_sb[:, tt, 512 * ho:512 * (ho + 1)],
                       C[:, 512 * ti:512 * (ti + 1)], ALU.add)
                    pub("vector", "xadd")
        wo_end_xadd = cnt["xadd"]
        wo_end_pep = cnt["pep"]
        if skip_wo:
            wt("sync", "ooev", cnt["ooev"])
            wt("sync", "pepv", cnt["pepv"])
            wt("sync", "cc", cnt["cc"])
            wt("sync", "pq2", cnt["pq2"])
            wt("sync", "kvout", cnt["kvout"])
    else:
        wo_end_xadd, wo_end_pep = ffn1_xadd, 0
        attn_end_pepv = attn_end_ptm = 0

    if "ffn2" in phases:
        emit_ln([("xadd", wo_end_xadd), ("dx", 80)],
                [("pesc", cnt["pesc"]), ("pepv", attn_end_pepv),
                 ("ptm", attn_end_ptm), ("pep", wo_end_pep)])
        emit_ffn(Gg, Gu, Gd, [("pesc", cnt["pesc"]), ("pep", wo_end_pep),
                              ("cc", cc_w)])

    if dbg == "attn":
        wt("sync", "ooev", cnt["ooev"])
        dma("sync", dat_oT, xnT[:].rearrange("p a b -> p (a b)"), "dout")
        wt("sync", "cc", cnt["cc"])
        dma("sync", dat_kg, KG, "dout")
        dma("sync", dat_vg, VG, "dout")

    if dbg == "tail":
        wt("sync", "xadd", cnt["xadd"])
        dma("sync", dcg, Gg[0:SLICE, :], "dout")
        dma("sync", dcd, cg, "dout")
        dma("sync", dgg, cu, "dout")
    wt("sync", "xadd", cnt["xadd"])
    dma("sync", out_t.rearrange("(t p) h -> p t h", p=128), x_sb[:], "dout")
    wt("sync", "dout", cnt["dout"])

    with nc.Block() as blk:
        blk.sync(lambda e: [f(e) for f in q["sync"]])
        blk.tensor(lambda e: [f(e) for f in q["tensor"]])
        blk.vector(lambda e: [f(e) for f in q["vector"]])
        blk.scalar(lambda e: [f(e) for f in q["scalar"]])
        blk.gpsimd(lambda e: [f(e) for f in q["gpsimd"]])
    ctx.close()
    return nc


_NC = None


def _get_nc():
    global _NC
    if _NC is None:
        _NC = build()
    return _NC


def make_in_maps(L0_attn_x, L0_attn_Wq, L0_attn_Wk, L0_attn_Wv, L0_attn_Wo,
                 L0_ffn_gate_W_packed, L0_ffn_gate_scales,
                 L0_ffn_up_W_packed, L0_ffn_up_scales,
                 L0_ffn_down_W_packed, L0_ffn_down_scales):
    x = np.ascontiguousarray(np.asarray(L0_attn_x, dtype=np.float32))
    wq = np.ascontiguousarray(np.asarray(L0_attn_Wq, dtype=np.float32))
    wk = np.ascontiguousarray(np.asarray(L0_attn_Wk, dtype=np.float32))
    wv = np.ascontiguousarray(np.asarray(L0_attn_Wv, dtype=np.float32))
    wo = np.ascontiguousarray(np.asarray(L0_attn_Wo, dtype=np.float32))
    gpk = np.ascontiguousarray(np.asarray(L0_ffn_gate_W_packed, dtype=np.int32))
    gsc = np.ascontiguousarray(np.asarray(L0_ffn_gate_scales, dtype=np.float32))
    upk = np.ascontiguousarray(np.asarray(L0_ffn_up_W_packed, dtype=np.int32))
    usc = np.ascontiguousarray(np.asarray(L0_ffn_up_scales, dtype=np.float32))
    dpk = np.ascontiguousarray(np.asarray(L0_ffn_down_W_packed, dtype=np.int32))
    dsc = np.ascontiguousarray(np.asarray(L0_ffn_down_scales, dtype=np.float32))

    ident = np.eye(128, dtype=ml_dtypes.bfloat16)
    cst = np.tile(np.array([-4.0, -12.0, 1e-5, 2.0, 0.125, 0.5, -0.125, -8.0,
                            7.5, 0, 0, 0, 0, 0, 0, 0], np.float32), (128, 1))
    in_maps = []
    for c in range(8):
        b, qb = c // 4, c % 4
        p_ = np.arange(128)
        kidx = np.empty((128, 32), np.int32)
        vidx = np.empty((128, 32), np.int32)
        for kh_ in range(8):
            for kb_ in range(4):
                kidx[:, 4 * kh_ + kb_] = 2048 * b + 512 * kb_ + 64 * kh_ + (p_ % 64)
        for ch_ in range(32):
            vidx[:, ch_] = 4096 * b + 128 * ch_ + p_
        qpos = 1024 * qb + np.arange(1024)
        kpos = np.arange(4096)
        m = (kpos[None, :] <= qpos[:, None]).astype(np.float32)
        mkt = np.ascontiguousarray(
            m.reshape(8, 128, 8, 4, 128).transpose(0, 2, 4, 3, 1)
        ).astype(ml_dtypes.bfloat16)
        in_maps.append(dict(
            x=np.ascontiguousarray(x[b, 1024 * qb:1024 * (qb + 1), :]),
            wq=wq, wk=wk, wv=wv, wo=wo,
            gpk=np.ascontiguousarray(gpk[2048 * c:2048 * (c + 1), :]),
            gsc=np.ascontiguousarray(gsc[2048 * c:2048 * (c + 1), :]),
            upk=np.ascontiguousarray(upk[2048 * c:2048 * (c + 1), :]),
            usc=np.ascontiguousarray(usc[2048 * c:2048 * (c + 1), :]),
            dpk=np.ascontiguousarray(dpk[:, 1024 * c:1024 * (c + 1)]),
            dsc=np.ascontiguousarray(dsc[:, 64 * c:64 * (c + 1)]),
            mkt=mkt, ident=ident, cst=cst,
            kidx=kidx, vidx=vidx,
        ))

    return in_maps


def assemble(results):
    out = np.empty((2, 4096, 2048), dtype=np.float32)
    for c in range(8):
        b, qb = c // 4, c % 4
        out[b, 1024 * qb:1024 * (qb + 1), :] = results[c]["out"]
    return out


def kernel(**inputs):
    in_maps = make_in_maps(**inputs)
    res = run_bass_kernel_spmd(_get_nc(), in_maps, core_ids=list(range(8)))
    return assemble(res.results)
